# version 04 (final)
# baseline (speedup 1.0000x reference)
import numpy as np
import ml_dtypes
import concourse.bass as bass
import concourse.mybir as mybir
from concourse.bass_utils import run_bass_kernel_spmd
from contextlib import ExitStack

F32 = mybir.dt.float32
BF16 = mybir.dt.bfloat16
U16 = mybir.dt.uint16
AF = mybir.ActivationFunctionType
ALU = mybir.AluOpType
AX = mybir.AxisListType

NCORES = 8
NB = 4
S = 2048
D = 1024
I3 = 3072
TOK = NB * S
NTT = TOK // 512
U16B = 32768.5
EPS = 1e-5

C_EPS, C_QM, C_SEVEN, C_U16B = 0, 1, 2, 3


class Sched:
    def __init__(self, nc):
        self.nc = nc
        self.ops = {e: [] for e in ("tensor", "vector", "scalar", "gpsimd", "sync")}
        self.sems = {}
        self.C = {}
        self.es = ExitStack()

    def sem(self, name):
        self.sems[name] = self.es.enter_context(self.nc.semaphore(name))
        self.C[name] = 0

    def op(self, engine, fn, wait=(), inc=()):
        for s, n in inc:
            self.C[s] += n
        self.seq = getattr(self, "seq", 0) + 1
        self.ops[engine].append((self.seq, fn, [w for w in wait if w], list(inc)))

    def emit(self, block, cut=None):
        def runner(name):
            def run(eng):
                for seq, fn, wait, inc in self.ops[name]:
                    if cut is not None and seq > cut:
                        break
                    for s, v in wait:
                        eng.wait_ge(self.sems[s], v)
                    ins = fn(eng)
                    if inc:
                        if ins is None:
                            ins = eng.nop()
                        for s, n in inc:
                            ins = ins.then_inc(self.sems[s], n)
            return run
        block.tensor(runner("tensor"))
        block.vector(runner("vector"))
        block.scalar(runner("scalar"))
        block.gpsimd(runner("gpsimd"))
        block.sync(runner("sync"))


def build(dev=False, cut_phase=None):
    nc = bass.Bass("TRN2", target_bir_lowering=False)

    xin = nc.dram_tensor("x", [NB, S, D], F32, kind="ExternalInput").ap()
    Wq = nc.dram_tensor("Wq", [D, D], F32, kind="ExternalInput").ap()
    Wk = nc.dram_tensor("Wk", [D, D], F32, kind="ExternalInput").ap()
    Wv = nc.dram_tensor("Wv", [D, D], F32, kind="ExternalInput").ap()
    Wo = nc.dram_tensor("Wo", [D, D], F32, kind="ExternalInput").ap()
    bq_in = nc.dram_tensor("bq", [D], F32, kind="ExternalInput").ap()
    bk_in = nc.dram_tensor("bk", [D], F32, kind="ExternalInput").ap()
    bv_in = nc.dram_tensor("bv", [D], F32, kind="ExternalInput").ap()
    bo_in = nc.dram_tensor("bo", [D], F32, kind="ExternalInput").ap()
    Wup = nc.dram_tensor("Wup", [I3, D], F32, kind="ExternalInput").ap()
    Wgate = nc.dram_tensor("Wgate", [I3, D], F32, kind="ExternalInput").ap()
    Wdown = nc.dram_tensor("Wdown", [D, I3], F32, kind="ExternalInput").ap()
    ident_in = nc.dram_tensor("ident", [128, 128], F32, kind="ExternalInput").ap()
    ones128_in = nc.dram_tensor("ones128", [128, 128], F32, kind="ExternalInput").ap()
    onesrow_in = nc.dram_tensor("onesrow", [1, 128], F32, kind="ExternalInput").ap()
    consts_in = nc.dram_tensor("consts", [128, 8], F32, kind="ExternalInput").ap()
    out = nc.dram_tensor("out", [NB, S, D], F32, kind="ExternalOutput").ap()

    dbg = "ExternalOutput" if dev else "Internal"
    x1_hbm = nc.dram_tensor("x1_hbm", [TOK, D], F32, kind=dbg).ap()
    up_hbm = nc.dram_tensor("up_hbm", [I3, TOK], F32, kind=dbg).ap()
    gate_hbm = nc.dram_tensor("gate_hbm", [I3, TOK], F32, kind=dbg).ap()
    t_hbm = nc.dram_tensor("t_hbm", [I3, TOK], F32, kind=dbg).ap()
    op_hbm = nc.dram_tensor("op_hbm", [I3, TOK], F32, kind=dbg).ap()
    scal_dbg = nc.dram_tensor("scal_dbg", [128, 16], F32, kind=dbg).ap()
    wdown_hbm = nc.dram_tensor("wdown_hbm", [24, 128, D], BF16, kind="Internal").ap()
    cch = [(nc.dram_tensor(f"cc_in{i}", [1, 8], F32),
            nc.dram_tensor(f"cc_out{i}", [1, 8], F32)) for i in range(4)]

    sbA = lambda n, s, d=F32: nc.alloc_sbuf_tensor(n, s, d).ap()
    ident_bf = sbA("ident_bf", [128, 128], BF16)
    ident_f32 = sbA("ident_f32", [128, 128])
    ones128_bf = sbA("ones128_bf", [128, 128], BF16)
    onesrow_bf = sbA("onesrow_bf", [1, 128], BF16)
    onesrow_f32 = sbA("onesrow_f32", [1, 128])
    onescol_bf = sbA("onescol_bf", [128, 1], BF16)
    consts = sbA("consts_sb", [128, 8])
    bqc = sbA("bqc", [128, 8])
    bkc = sbA("bkc", [128, 8])
    bv_row = sbA("bv_row", [1, D], BF16)
    bo_row = sbA("bo_row", [1, D], BF16)
    scal = sbA("scal", [128, 16])
    m1loc = sbA("m1loc", [128, 1])
    mupl = sbA("mupl", [128, 1])
    mgatel = sbA("mgatel", [128, 1])
    mtl = sbA("mtl", [128, 1])
    mol = sbA("mol", [128, 1])
    srow = sbA("srow", [1, 16])
    rec327 = sbA("rec327", [128, 1])
    rec7 = sbA("rec7", [128, 1])
    colA = sbA("colA", [128, 1])
    colB = sbA("colB", [128, 1])
    lncol = sbA("lncol", [128, 8])

    S0, R0 = 0, 1
    SU, RU, SG, RG = 2, 3, 4, 5
    S4, R4 = 6, 7
    S5, R5 = 8, 9
    S4U, NB4U = 10, 11
    NB0, NBG, NB5 = 12, 13, 14

    bankA = nc.alloc_psum_tensor("bankA", [128, 512], F32).ap()
    bankB = nc.alloc_psum_tensor("bankB", [128, 512], F32).ap()
    bankC = nc.alloc_psum_tensor("bankC", [128, 1024], F32).ap()
    bankD = nc.alloc_psum_tensor("bankD", [128, 1024], F32).ap()
    bankT = nc.alloc_psum_tensor("bankT", [128, 1024], BF16).ap()
    bankU = nc.alloc_psum_tensor("bankU", [128, 1024], BF16).ap()

    Q = Sched(nc)
    Q.marks = {}
    Q.seq = 0
    for s in ("ld", "st", "gp", "mm", "ve", "se", "gc", "ccs", "bar",
              "gx0", "gx1", "gwA", "gwB"):
        Q.sem(s)
    C = Q.C

    def T(fn, **kw): Q.op("tensor", fn, **kw)
    def V(fn, **kw): Q.op("vector", fn, **kw)
    def SC(fn, **kw): Q.op("scalar", fn, **kw)
    def G(fn, **kw): Q.op("gpsimd", fn, **kw)
    def SY(fn, **kw): Q.op("sync", fn, **kw)

    def barrier():
        tgt = C["bar"] + 5
        for e in ("tensor", "vector", "scalar", "gpsimd", "sync"):
            Q.op(e, lambda eng: eng.nop(), inc=[("bar", 1)])
        for e in ("tensor", "vector", "scalar", "gpsimd", "sync"):
            Q.op(e, lambda eng: None, wait=[("bar", tgt)])

    G(lambda g: g.dma_start(out=ident_bf, in_=ident_in), inc=[("gp", 16)])
    G(lambda g: g.dma_start(out=ones128_bf, in_=ones128_in), inc=[("gp", 16)])
    G(lambda g: g.dma_start(out=onesrow_bf, in_=onesrow_in), inc=[("gp", 16)])
    G(lambda g: g.dma_start(out=onescol_bf, in_=ones128_in[:, 0:1], allow_slow_non_contiguous=True), inc=[("gp", 16)])
    G(lambda g: g.dma_start(out=bv_row, in_=bv_in.rearrange("(a d) -> a d", a=1)), inc=[("gp", 16)])
    G(lambda g: g.dma_start(out=bo_row, in_=bo_in.rearrange("(a d) -> a d", a=1)), inc=[("gp", 16)])
    SY(lambda s: s.dma_start(out=consts, in_=consts_in), inc=[("ld", 16)])
    SY(lambda s: s.dma_start(out=bqc, in_=bq_in.rearrange("(c p) -> p c", p=128), allow_slow_non_contiguous=True), inc=[("ld", 16)])
    SY(lambda s: s.dma_start(out=bkc, in_=bk_in.rearrange("(c p) -> p c", p=128), allow_slow_non_contiguous=True), inc=[("ld", 16)])
    SY(lambda s: s.dma_start(out=ident_f32, in_=ident_in), inc=[("ld", 16)])
    SY(lambda s: s.dma_start(out=onesrow_f32, in_=onesrow_in), inc=[("ld", 16)])
    ld_consts = C["ld"]

    V(lambda v: v.reciprocal(colA, consts[:, C_QM:C_QM + 1]),
      wait=[("ld", ld_consts)], inc=())
    V(lambda v: v.reciprocal(colB, consts[:, C_SEVEN:C_SEVEN + 1]), inc=())
    V(lambda v: v.drain(), inc=[("ve", 2)])
    G(lambda g: g.tensor_copy(rec327, colA), wait=[("ve", C["ve"])], inc=[("gc", 1)])
    G(lambda g: g.tensor_copy(rec7, colB), inc=[("gc", 1)])
    gc_rec = C["gc"]
    for col in (m1loc, mupl, mgatel, mtl, mol):
        V((lambda c: lambda v: v.memset(c, 0.0))(col), inc=[("ve", 1)])
    Q.op("vector", lambda e: None, wait=[("ld", ld_consts)])
    Q.op("scalar", lambda e: None, wait=[("ld", ld_consts)])
    Q.op("tensor", lambda e: None, wait=[("ld", ld_consts)])
    Q.op("gpsimd", lambda e: None, wait=[("ld", ld_consts)])
    ve_init = C["ve"]

    attn_scope = ExitStack()
    sbT = lambda n, s, d=F32: attn_scope.enter_context(nc.sbuf_tensor(n, s, d)).ap()
    wA = sbT("wA", [128, 8 * D], BF16)
    wB = sbT("wB", [128, 8 * D], BF16)
    x_tm = sbT("x_tm", [128, 2 * D], BF16)
    x_fm = sbT("x_fm", [128, 8 * S], BF16)
    k_fm = sbT("k_fm", [128, 8 * S], BF16)
    v_tm = sbT("v_tm", [128, 16 * D], BF16)
    ksq_bc = sbT("ksq_bc", [128, S], BF16)
    q_fm = sbT("q_fm", [128, 8 * 512], BF16)
    q2c = sbT("q2c", [128, 8 * 512], BF16)
    k2c = q2c[:, 0:512]
    nqsq = sbT("nqsq", [128, 4])
    spre = sbT("spre", [128, S])
    p_bf = sbT("p_bf", [128, S], BF16)
    pt_bf = sbT("pt_bf", [128, 16 * 256], BF16)
    av_bf = sbT("av_bf", [128, 8 * 256], BF16)
    rowmx = sbT("rowmx", [128, 1])
    rowsum = sbT("rowsum", [128, 1])
    rowrec = sbT("rowrec", [128, 1])
    xres = sbT("xres", [128, D])
    r1t = sbT("r1t", [128, D])
    x1t = xres

    def load_weight(slot, whbm, wait_mm):
        sem = "gwA" if slot is wA else "gwB"
        for dc in range(8):
            G((lambda w_, h_, dc_: lambda g: g.dma_start(
                out=w_[:, dc_ * D:(dc_ + 1) * D],
                in_=h_[dc_ * 128:(dc_ + 1) * 128, :]))(slot, whbm, dc),
              wait=([("mm", wait_mm)] if (dc == 0 and wait_mm) else ()),
              inc=[(sem, 16)])
        return C[sem]

    ve_ln_last = ve_init
    se_evac = [0, 0]
    mm_lastQ = 0
    mm_lastO = 0
    for b in range(NB):
        gp_wk = load_weight(wA, Wk, mm_lastQ)
        gp_wv = load_weight(wB, Wv, mm_lastO)
        mm_xtr = [0, 0]
        ve_xevac = [ve_init, ve_init]
        for tt in range(16):
            j = tt % 2
            xt = x_tm[:, j * D:(j + 1) * D]
            w = []
            if tt >= 2:
                w.append(("mm", mm_xtr[j]))
            if b == 0 and tt == 0:
                w.append(("ve", ve_init))
            if b > 0 and tt == 0:
                w.append(("ve", ve_ln_last))
            gxs = "gx0" if j == 0 else "gx1"
            G((lambda b_, tt_, xt_: lambda g: g.dma_start(
                out=xt_, in_=xin[b_, tt_ * 128:(tt_ + 1) * 128, :]))(b, tt, xt),
              wait=w, inc=[(gxs, 16)])
            gp_now = C[gxs]
            bank = bankT if j == 0 else bankU
            for dc in range(8):
                T((lambda xt_, dc_, bank_, s_, v_: lambda t: t.transpose(
                    bank_[:, dc_ * 128:(dc_ + 1) * 128],
                    xt_[:, dc_ * 128:(dc_ + 1) * 128], ident_bf))(xt, dc, bank, gxs, gp_now),
                  wait=([(gxs, gp_now)] if dc == 0 else []) +
                       ([("ve", ve_xevac[j])] if (dc == 0 and tt >= 2) else []),
                  inc=[("mm", 1)])
            mm_xtr[j] = C["mm"]
            V((lambda tt_, bank_: lambda v: v.tensor_copy(
                x_fm.rearrange("p (c s) -> p c s", c=8)[:, :, tt_ * 128:(tt_ + 1) * 128],
                bank_.rearrange("p (c s) -> p c s", c=8)))(tt, bank),
              wait=[("mm", mm_xtr[j])], inc=[("ve", 1)])
            ve_xevac[j] = C["ve"]
        ve_xfm = C["ve"]

        mm_ksq = 0
        ve_ksqev = C["ve"]
        for kt in range(4):
            for dc in range(8):
                i = kt * 8 + dc
                bank = bankA if i % 2 == 0 else bankB
                for ic in range(8):
                    w = []
                    if i == 0 and ic == 0:
                        w = [("ve", ve_xfm), ("gwA", gp_wk)]
                    if ic == 0 and se_evac[i % 2]:
                        w = w + [("se", se_evac[i % 2])]
                    T((lambda kt_, dc_, ic_, bank_: lambda t: t.matmul(
                        bank_,
                        wA[:, ic_ * D + dc_ * 128: ic_ * D + dc_ * 128 + 128],
                        x_fm[:, ic_ * S + kt_ * 512: ic_ * S + kt_ * 512 + 512],
                        start=(ic_ == 0), stop=(ic_ == 7)))(kt, dc, ic, bank),
                      wait=w, inc=[("mm", 1)] if ic == 7 else ())
                mm_k = C["mm"]
                SC((lambda kt_, dc_, bank_: lambda s: s.activation(
                    k_fm[:, dc_ * S + kt_ * 512: dc_ * S + kt_ * 512 + 512],
                    bank_, AF.Identity, bias=bkc[:, dc_:dc_ + 1], scale=1.0))(kt, dc, bank),
                   wait=[("mm", mm_k)], inc=[("se", 1)])
                SC((lambda dc_, bank_: lambda s: s.activation(
                    k2c, bank_, AF.Square, bias=bkc[:, dc_:dc_ + 1], scale=1.0))(dc, bank),
                   wait=([("mm", mm_ksq)] if mm_ksq else ()),
                   inc=[("se", 1)])
                se_evac[i % 2] = C["se"]
                T((lambda dc_: lambda t: t.matmul(
                    bankC[:, 0:512], ones128_bf, k2c,
                    start=(dc_ == 0), stop=(dc_ == 7)))(dc),
                  wait=[("se", C["se"])] + ([("ve", ve_ksqev)] if dc == 0 else []),
                  inc=[("mm", 1)])
                mm_ksq = C["mm"]
            V((lambda kt_: lambda v: v.tensor_copy(
                ksq_bc[:, kt_ * 512:(kt_ + 1) * 512], bankC[:, 0:512]))(kt),
              wait=[("mm", mm_ksq)], inc=[("ve", 1)])
            ve_ksqev = C["ve"]
        ve_ksq = C["ve"]
        gp_wq = load_weight(wA, Wq, C["mm"])

        for tc in range(16):
            for dt in range(2):
                i = tc * 2 + dt
                bank = bankA if i % 2 == 0 else bankB
                T((lambda dt_, bank_: lambda t: t.matmul(
                    bank_, onesrow_bf[0:1, :], bv_row[0:1, dt_ * 512:(dt_ + 1) * 512],
                    start=True, stop=False))(dt, bank),
                  wait=([("se", se_evac[i % 2])] if se_evac[i % 2] else []) +
                       ([("gwB", gp_wv)] if i == 0 else []),
                  inc=())
                for ic in range(8):
                    T((lambda tc_, dt_, ic_, bank_: lambda t: t.matmul(
                        bank_,
                        x_fm[:, ic_ * S + tc_ * 128: ic_ * S + (tc_ + 1) * 128],
                        wB[:, ic_ * D + dt_ * 512: ic_ * D + dt_ * 512 + 512],
                        start=False, stop=(ic_ == 7)))(tc, dt, ic, bank),
                      inc=[("mm", 1)] if ic == 7 else ())
                SC((lambda tc_, dt_, bank_: lambda s: s.activation(
                    v_tm[:, tc_ * D + dt_ * 512: tc_ * D + dt_ * 512 + 512],
                    bank_, AF.Identity, bias=0.0, scale=1.0))(tc, dt, bank),
                   wait=[("mm", C["mm"])], inc=[("se", 1)])
                se_evac[i % 2] = C["se"]

        gp_wo = load_weight(wB, Wo, C["mm"])

        for g in range(4):
            mm_qsq = 0
            for dc in range(8):
                bank = bankA if dc % 2 == 0 else bankB
                for ic in range(8):
                    w = []
                    if dc == 0 and ic == 0:
                        w = [("ve", ve_ln_last), ("gwA", gp_wq)]
                    if ic == 0 and se_evac[dc % 2]:
                        w = w + [("se", se_evac[dc % 2])]
                    T((lambda g_, dc_, ic_, bank_: lambda t: t.matmul(
                        bank_,
                        wA[:, ic_ * D + dc_ * 128: ic_ * D + dc_ * 128 + 128],
                        x_fm[:, ic_ * S + g_ * 512: ic_ * S + g_ * 512 + 512],
                        start=(ic_ == 0), stop=(ic_ == 7)))(g, dc, ic, bank),
                      wait=w, inc=[("mm", 1)] if ic == 7 else ())
                mm_q = C["mm"]
                SC((lambda dc_, bank_: lambda s: s.activation(
                    q_fm[:, dc_ * 512:(dc_ + 1) * 512], bank_, AF.Identity,
                    bias=bqc[:, dc_:dc_ + 1], scale=1.0))(dc, bank),
                   wait=[("mm", mm_q)], inc=[("se", 1)])
                SC((lambda dc_, bank_: lambda s: s.activation(
                    q2c[:, dc_ * 512:(dc_ + 1) * 512], bank_, AF.Square,
                    bias=bqc[:, dc_:dc_ + 1], scale=1.0))(dc, bank),
                   wait=([("mm", mm_qsq)] if (mm_qsq and dc == 0) else ()),
                   inc=[("se", 1)])
                se_evac[dc % 2] = C["se"]
            se_q2 = C["se"]
            for qt in range(4):
                for dc in range(8):
                    T((lambda qt_, dc_: lambda t: t.matmul(
                        bankC[:, 512 + qt_ * 8: 513 + qt_ * 8],
                        q2c[:, dc_ * 512 + qt_ * 128: dc_ * 512 + (qt_ + 1) * 128],
                        onescol_bf, start=(dc_ == 0), stop=(dc_ == 7)))(qt, dc),
                      wait=[("se", se_q2)] if (qt == 0 and dc == 0) else (),
                      inc=[("mm", 1)])
            mm_qsq = C["mm"]
            SC(lambda s: s.activation(nqsq, bankC[:, 512:544:8], AF.Identity,
                                      bias=0.0, scale=-1.0),
               wait=[("mm", mm_qsq)], inc=[("se", 1)])
            se_nqsq = C["se"]

            for half in range(2):
                for qh in range(2):
                    qt = half * 2 + qh
                    ve_qk = [C["ve"], C["ve"]]
                    for kb in range(4):
                        bank = bankD[:, 0:512] if kb % 2 == 0 else bankD[:, 512:1024]
                        for dc in range(8):
                            w = []
                            if qt == 0 and half == 0 and kb == 0 and dc == 0:
                                w = [("se", se_nqsq), ("ve", ve_ksq)]
                            if dc == 0:
                                w = w + [("ve", ve_qk[kb % 2])]
                            T((lambda qt_, kb_, dc_, bank_: lambda t: t.matmul(
                                bank_,
                                q_fm[:, dc_ * 512 + qt_ * 128: dc_ * 512 + (qt_ + 1) * 128],
                                k_fm[:, dc_ * S + kb_ * 512: dc_ * S + kb_ * 512 + 512],
                                start=(dc_ == 0), stop=(dc_ == 7)))(qt, kb, dc, bank),
                              wait=w, inc=[("mm", 1)] if dc == 7 else ())
                        V((lambda kb_, bank_: lambda v: v.scalar_tensor_tensor(
                            spre[:, kb_ * 512:(kb_ + 1) * 512], bank_, 2.0,
                            ksq_bc[:, kb_ * 512:(kb_ + 1) * 512],
                            ALU.mult, ALU.subtract))(kb, bank),
                          wait=[("mm", C["mm"])], inc=[("ve", 1)])
                        ve_qk[kb % 2] = C["ve"]
                    SC((lambda qt_: lambda s: s.activation(
                        spre, spre, AF.Exp, bias=nqsq[:, qt_:qt_ + 1], scale=1.0))(qt),
                       wait=[("ve", C["ve"])], inc=[("se", 1)])
                    V(lambda v: v.tensor_reduce(rowmx, spre, AX.X, ALU.max, negate=True),
                      wait=[("se", C["se"])], inc=[("ve", 1)])
                    SC(lambda s: s.activation(spre, spre, AF.Exp, bias=rowmx,
                                              scale=1.0, accum_out=rowsum),
                       wait=[("ve", C["ve"])], inc=[("se", 1)])
                    V(lambda v: v.reciprocal(rowrec, rowsum),
                      wait=[("se", C["se"])], inc=())
                    V(lambda v: v.drain(), inc=[("ve", 1)])
                    SC(lambda s: s.activation(p_bf, spre, AF.Identity,
                                              bias=0.0, scale=rowrec),
                       wait=[("ve", C["ve"])], inc=[("se", 1)])
                    se_pbf = C["se"]
                    ve_ptev = [C["ve"], C["ve"]]
                    for tc in range(16):
                        bank = bankT if tc % 2 == 0 else bankU
                        T((lambda tc_, bank_: lambda t: t.transpose(
                            bank_[:, 0:128], p_bf[:, tc_ * 128:(tc_ + 1) * 128],
                            ident_bf))(tc, bank),
                          wait=([("se", se_pbf)] if tc == 0 else []) +
                               [("ve", ve_ptev[tc % 2])],
                          inc=[("mm", 1)])
                        V((lambda qh_, tc_, bank_: lambda v: v.tensor_copy(
                            pt_bf[:, tc_ * 256 + qh_ * 128: tc_ * 256 + (qh_ + 1) * 128],
                            bank_[:, 0:128]))(qh, tc, bank),
                          wait=[("mm", C["mm"])], inc=[("ve", 1)])
                        ve_ptev[tc % 2] = C["ve"]
                ve_pt = C["ve"]

                for dc in range(8):
                    bank = bankA if dc % 2 == 0 else bankB
                    for tc in range(16):
                        w = []
                        if dc == 0 and tc == 0:
                            w = [("ve", ve_pt)]
                        if tc == 0 and se_evac[dc % 2]:
                            w = w + [("se", se_evac[dc % 2])]
                        T((lambda dc_, tc_, bank_: lambda t: t.matmul(
                            bank_[:, 0:256],
                            v_tm[:, tc_ * D + dc_ * 128: tc_ * D + (dc_ + 1) * 128],
                            pt_bf[:, tc_ * 256:(tc_ + 1) * 256],
                            start=(tc_ == 0), stop=(tc_ == 15)))(dc, tc, bank),
                          wait=w, inc=[("mm", 1)] if tc == 15 else ())
                    SC((lambda dc_, bank_: lambda s: s.activation(
                        av_bf[:, dc_ * 256:(dc_ + 1) * 256], bank_[:, 0:256],
                        AF.Identity, bias=0.0, scale=1.0))(dc, bank),
                       wait=[("mm", C["mm"])], inc=[("se", 1)])
                    se_evac[dc % 2] = C["se"]
                se_av = C["se"]

                for qh in range(2):
                    tok0 = b * S + g * 512 + half * 256 + qh * 128
                    SY((lambda b_, off_: lambda s: s.dma_start(
                        out=xres, in_=xin[b_, off_:off_ + 128, :]))(
                            b, g * 512 + half * 256 + qh * 128),
                       wait=[("ve", C["ve"]), ("st", C["st"])], inc=[("ld", 16)])
                    ld_xres = C["ld"]
                    for dt in range(2):
                        T((lambda dt_: lambda t: t.matmul(
                            bankC[:, dt_ * 512:(dt_ + 1) * 512], onesrow_bf[0:1, :],
                            bo_row[0:1, dt_ * 512:(dt_ + 1) * 512],
                            start=True, stop=False))(dt),
                          wait=([("se", se_av), ("ve", ve_ln_last), ("gwB", gp_wo)]
                                if dt == 0 else ()),
                          inc=())
                        for dc in range(8):
                            T((lambda qh_, dt_, dc_: lambda t: t.matmul(
                                bankC[:, dt_ * 512:(dt_ + 1) * 512],
                                av_bf[:, dc_ * 256 + qh_ * 128: dc_ * 256 + (qh_ + 1) * 128],
                                wB[:, dc_ * D + dt_ * 512: dc_ * D + dt_ * 512 + 512],
                                start=False, stop=(dc_ == 7)))(qh, dt, dc),
                              inc=[("mm", 1)] if dc == 7 else ())
                    V(lambda v: v.scalar_tensor_tensor(
                        r1t, bankC, 1.0, xres, ALU.mult, ALU.add),
                      wait=[("mm", C["mm"]), ("ld", ld_xres)], inc=[("ve", 1)])
                    V(lambda v: v.reduce_sum(lncol[:, 0:1], r1t, AX.X), inc=())
                    V(lambda v: v.drain(), inc=[("ve", 1)])
                    SC(lambda s: s.activation(xres, r1t, AF.Square, bias=0.0,
                                              scale=1.0),
                       wait=[("ve", C["ve"])], inc=[("se", 1)])
                    V(lambda v: v.reduce_sum(lncol[:, 2:3], xres, AX.X),
                      wait=[("se", C["se"])], inc=())
                    V(lambda v: v.drain(), inc=())
                    V(lambda v: v.tensor_scalar_mul(lncol[:, 1:2], lncol[:, 0:1], 1.0 / D), inc=())
                    V(lambda v: v.drain(), inc=())
                    V(lambda v: v.tensor_scalar(lncol[:, 3:4], lncol[:, 2:3], 1.0 / D,
                                                consts[:, C_EPS:C_EPS + 1], ALU.mult, ALU.add),
                      inc=())
                    V(lambda v: v.drain(), inc=())
                    V(lambda v: v.tensor_mul(colA, lncol[:, 1:2], lncol[:, 1:2]), inc=())
                    V(lambda v: v.drain(), inc=())
                    V(lambda v: v.tensor_tensor(lncol[:, 3:4], lncol[:, 3:4], colA,
                                                ALU.subtract), inc=())
                    V(lambda v: v.drain(), inc=[("ve", 1)])
                    SC(lambda s: s.activation(lncol[:, 4:5], lncol[:, 3:4], AF.Ln,
                                              bias=0.0, scale=1.0),
                       wait=[("ve", C["ve"])], inc=())
                    SC(lambda s: s.activation(lncol[:, 4:5], lncol[:, 4:5], AF.Exp,
                                              bias=0.0, scale=-0.5), inc=[("se", 1)])
                    V(lambda v: v.engine_nop(), wait=[("se", C["se"])], inc=())
                    for _ in range(2):
                        V(lambda v: v.tensor_mul(colA, lncol[:, 4:5], lncol[:, 4:5]), inc=())
                        V(lambda v: v.drain(), inc=())
                        V(lambda v: v.tensor_tensor(colA, colA, lncol[:, 3:4], ALU.mult), inc=())
                        V(lambda v: v.drain(), inc=())
                        V(lambda v: v.tensor_scalar(colA, colA, -0.5, 1.5, ALU.mult, ALU.add), inc=())
                        V(lambda v: v.drain(), inc=())
                        V(lambda v: v.tensor_tensor(lncol[:, 4:5], lncol[:, 4:5], colA, ALU.mult), inc=())
                        V(lambda v: v.drain(), inc=())
                    V(lambda v: v.tensor_mul(colB, lncol[:, 1:2], lncol[:, 4:5]), inc=())
                    V(lambda v: v.drain(), inc=())
                    V(lambda v: v.tensor_scalar_mul(lncol[:, 5:6], colB, -1.0),
                      inc=())
                    V(lambda v: v.drain(), inc=[("ve", 1)])
                    SC(lambda s: s.activation(x1t, r1t, AF.Identity,
                                              bias=lncol[:, 5:6], scale=lncol[:, 4:5]),
                       wait=[("ve", C["ve"])], inc=[("se", 1)])
                    SC((lambda tok0_: lambda s: s.dma_start(
                        out=x1_hbm[tok0_:tok0_ + 128, :], in_=x1t))(tok0),
                       inc=[("st", 16)])
                    V(lambda v: v.tensor_reduce(colA, x1t, AX.X, ALU.max,
                                                apply_absolute_value=True),
                      wait=[("se", C["se"]), ("st", C["st"])], inc=())
                    V(lambda v: v.drain(), inc=())
                    V(lambda v: v.tensor_tensor(m1loc, m1loc, colA, ALU.max),
                      inc=())
                    V(lambda v: v.drain(), inc=[("ve", 1)])
                    ve_ln_last = C["ve"]
        mm_lastQ = C["mm"]
        mm_lastO = C["mm"]

    if dev:
        dbg_specs = [("dbg_ln", lncol, [128, 8], F32), ("dbg_r1", r1t, [128, D], F32),
                     ("dbg_q", q_fm, [128, 4096], BF16), ("dbg_k", k_fm[:, 0:2048], [128, 2048], BF16),
                     ("dbg_v", v_tm[:, 0:1024], [128, 1024], BF16),
                     ("dbg_sp", spre, [128, 2048], F32), ("dbg_pb", p_bf, [128, 2048], BF16),
                     ("dbg_av", av_bf, [128, 2048], BF16), ("dbg_xf", x_fm[:, 0:2048], [128, 2048], BF16),
                     ("dbg_ks", ksq_bc, [128, 2048], BF16),
                     ("dbg_nq", nqsq, [128, 4], F32), ("dbg_rs", rowsum, [128, 1], F32)]
        for nm, ap_, shp, dt_ in dbg_specs:
            dbg_t = nc.dram_tensor(nm, shp, dt_, kind="ExternalOutput").ap()
            SC((lambda d_, a_: lambda s: s.dma_start(out=d_, in_=a_))(dbg_t, ap_),
               wait=[("ve", C["ve"]), ("se", C["se"])], inc=[("st", 16)])
    Q.marks["A"] = Q.seq
    barrier()

    def col_to_srow(colsrc, slot, first_wait=()):
        T((lambda cs: lambda t: t.transpose(
            bankD[0:1, 0:128], cs, ident_f32))(colsrc),
          wait=list(first_wait) + [("ve", C["ve"])], inc=[("mm", 1)])
        V((lambda sl: lambda v: v.tensor_reduce(
            srow[0:1, sl:sl + 1], bankD[0:1, 0:128], AX.X, ALU.max))(slot),
          wait=[("mm", C["mm"])], inc=())
        V(lambda v: v.drain(), inc=[("ve", 1)])

    def allreduce(idx, nvals):
        ci, co = cch[idx]
        G((lambda ci_, n_: lambda g: g.dma_start(
            out=ci_.ap()[0:1, 0:n_], in_=srow[0:1, 0:n_]))(ci, nvals),
          wait=[("ve", C["ve"])], inc=[("gp", 16)])
        G((lambda ci_, co_: lambda g: g.collective_compute(
            "AllReduce", ALU.max, replica_groups=[list(range(NCORES))],
            ins=[ci_.ap().opt()], outs=[co_.ap().opt()]))(ci, co),
          wait=[("gp", C["gp"])], inc=[("ccs", 1)])
        G((lambda co_, n_: lambda g: g.dma_start(
            out=srow[0:1, 8:8 + n_], in_=co_.ap()[0:1, 0:n_]))(co, nvals),
          wait=[("ccs", C["ccs"])], inc=[("gp", 16)])

    def bcast_scale(j, s_slot, r_slot, nb_slot=None):
        T((lambda j_: lambda t: t.matmul(
            bankD[:, 0:1], onesrow_f32[0:1, :], srow[0:1, 8 + j_:9 + j_],
            start=True, stop=True))(j),
          wait=[("gp", C["gp"]), ("ve", C["ve"])], inc=[("mm", 1)])
        V((lambda sl: lambda v: v.tensor_scalar(
            scal[:, sl:sl + 1], bankD[:, 0:1], rec327, None, ALU.mult))(s_slot),
          wait=[("mm", C["mm"]), ("gc", gc_rec)], inc=())
        V(lambda v: v.drain(), inc=())
        V((lambda sl: lambda v: v.tensor_scalar_max(
            scal[:, sl:sl + 1], scal[:, sl:sl + 1], 1e-12))(s_slot), inc=())
        V(lambda v: v.drain(), inc=())
        if nb_slot is not None:
            V((lambda sl, nb_: lambda v: v.tensor_scalar_mul(
                scal[:, nb_:nb_ + 1], scal[:, sl:sl + 1], -32768.0))(s_slot, nb_slot),
              inc=())
            V(lambda v: v.drain(), inc=())
        V((lambda sl, rl: lambda v: v.reciprocal(
            scal[:, rl:rl + 1], scal[:, sl:sl + 1]))(s_slot, r_slot),
          inc=())
        V(lambda v: v.drain(), inc=[("ve", 1)])

    col_to_srow(m1loc, 0)
    allreduce(0, 1)
    bcast_scale(0, S0, R0, NB0)

    attn_scope.close()
    ug_scope = ExitStack()
    sbUG = lambda n, s, d=F32: ug_scope.enter_context(nc.sbuf_tensor(n, s, d)).ap()
    wup_dq = sbUG("wup_dq", [128, 8 * I3], BF16)
    wgate_dq = sbUG("wgate_dq", [128, 8 * I3], BF16)
    ffn_scope = ExitStack()
    sbF = lambda n, s, d=F32: ffn_scope.enter_context(nc.sbuf_tensor(n, s, d, side="right")).ap()
    x1q_tm = sbF("x1q_tm", [128, 2 * D], BF16)
    x1q_fm = sbF("x1q_fm", [128, 8 * 512], BF16)
    f32a = sbF("f32a", [128, 512])
    f32b = sbF("f32b", [128, 512])
    f32c = sbF("f32c", [128, 512])
    f32d = sbF("f32d", [128, 512])
    u16a = sbF("u16a", [128, 512], U16)
    u16b = sbF("u16b", [128, 512], U16)
    y2t = sbF("y2t", [128, D])
    x1rd = sbF("x1rd", [128, D])
    r1f = sbF("r1f", [128, D])

    deq_scope = ExitStack()
    sbDQ = lambda n, s, d=F32: deq_scope.enter_context(nc.sbuf_tensor(n, s, d)).ap()
    wraw = sbDQ("wraw", [128, I3])
    wraw2 = sbDQ("wraw2", [128, I3])
    bmax = sbDQ("bmax", [128, 96])
    brec = sbDQ("brec", [128, 96])
    brec2 = sbDQ("brec2", [128, 96])
    bscale = sbDQ("bscale", [128, 96])
    wy = sbDQ("wy", [128, I3])
    wu16 = sbDQ("wu16", [128, I3], U16)
    wdq_f = sbDQ("wdq_f", [128, I3])
    wdq_b = sbDQ("wdq_b", [128, I3], BF16)
    stg_bf = sbDQ("stg_bf", [128, 256], BF16)
    gc_chain = [0, 0]

    def dequant(wsrc, wdst, noc, nic, to_hbm=False):
        nblk = nic * 128 // 32
        In = nic * 128
        for oc in range(noc):
            wr = wraw if oc % 2 == 0 else wraw2
            SY((lambda oc_, wr_, In_: lambda s: s.dma_start(
                out=wr_[:, 0:In_], in_=wsrc[oc_ * 128:(oc_ + 1) * 128, :]))(oc, wr, In),
               wait=[("gc", gc_chain[oc % 2])] if gc_chain[oc % 2] else (),
               inc=[("ld", 16)])
            ld_w = C["ld"]
            V((lambda wr_, nblk_: lambda v: v.tensor_reduce(
                bmax[:, 0:nblk_], wr_[:, 0:nblk_ * 32].rearrange("p (b e) -> p b e", e=32),
                AX.X, ALU.max, apply_absolute_value=True))(wr, nblk),
              wait=[("ld", ld_w), ("gc", C["gc"])], inc=())
            V(lambda v: v.drain(), inc=())
            V((lambda nblk_: lambda v: v.tensor_scalar_max(
                bmax[:, 0:nblk_], bmax[:, 0:nblk_], 7e-12))(nblk), inc=())
            V(lambda v: v.drain(), inc=())
            V((lambda nblk_: lambda v: v.reciprocal(
                brec[:, 0:nblk_], bmax[:, 0:nblk_]))(nblk), inc=())
            V(lambda v: v.drain(), inc=[("ve", 1)])
            G((lambda nblk_: lambda g: g.tensor_copy(
                brec2[:, 0:nblk_], brec[:, 0:nblk_]))(nblk),
              wait=[("ve", C["ve"])], inc=[("gc", 1)])
            G((lambda wr_, nblk_: lambda g: g.tensor_tensor(
                wy[:, 0:nblk_ * 32].rearrange("p (b e) -> p b e", e=32),
                wr_[:, 0:nblk_ * 32].rearrange("p (b e) -> p b e", e=32),
                brec2[:, 0:nblk_].unsqueeze(2).broadcast_to([128, nblk_, 32]),
                ALU.mult))(wr, nblk), inc=[("gc", 1)])
            V((lambda In_: lambda v: v.tensor_scalar(
                wy[:, 0:In_], wy[:, 0:In_], 7.0, 8.0, ALU.mult, ALU.add))(In),
              wait=[("gc", C["gc"])], inc=())
            V((lambda In_: lambda v: v.tensor_copy(wu16[:, 0:In_], wy[:, 0:In_]))(In), inc=())
            V((lambda nblk_: lambda v: v.tensor_scalar(
                bscale[:, 0:nblk_], bmax[:, 0:nblk_], rec7, None, ALU.mult))(nblk), inc=())
            V(lambda v: v.drain(), inc=())
            V((lambda In_: lambda v: v.tensor_copy(wdq_f[:, 0:In_], wu16[:, 0:In_]))(In), inc=())
            V((lambda In_: lambda v: v.tensor_tensor(
                wy[:, 0:In_], wy[:, 0:In_], wdq_f[:, 0:In_], ALU.subtract))(In), inc=())
            V((lambda In_: lambda v: v.tensor_single_scalar(
                wy[:, 0:In_], wy[:, 0:In_], 0.5, ALU.is_gt))(In), inc=())
            V((lambda In_: lambda v: v.tensor_tensor(
                wdq_f[:, 0:In_], wdq_f[:, 0:In_], wy[:, 0:In_], ALU.add))(In), inc=())
            V((lambda In_: lambda v: v.tensor_scalar(
                wdq_f[:, 0:In_], wdq_f[:, 0:In_], -8.0, None, ALU.add))(In),
              inc=[("ve", 1)])
            G((lambda nblk_: lambda g: g.tensor_tensor(
                wdq_f[:, 0:nblk_ * 32].rearrange("p (b e) -> p b e", e=32),
                wdq_f[:, 0:nblk_ * 32].rearrange("p (b e) -> p b e", e=32),
                bscale[:, 0:nblk_].unsqueeze(2).broadcast_to([128, nblk_, 32]),
                ALU.mult))(nblk),
              wait=[("ve", C["ve"])], inc=[("gc", 1)])
            V((lambda In_: lambda v: v.tensor_copy(wdq_b[:, 0:In_], wdq_f[:, 0:In_]))(In),
              wait=[("gc", C["gc"])], inc=[("ve", 1)])
            ve_wdq = C["ve"]
            ve_tr = [ve_wdq, ve_wdq]
            for ic in range(nic):
                bank = bankT if ic % 2 == 0 else bankU
                T((lambda ic_, bank_: lambda t: t.transpose(
                    bank_[:, 0:128], wdq_b[:, ic_ * 128:(ic_ + 1) * 128],
                    ident_bf))(ic, bank),
                  wait=([("ve", ve_wdq)] if ic == 0 else []) +
                       ([("ve", ve_tr[ic % 2])] if ic >= 2 else []),
                  inc=[("mm", 1)])
                if to_hbm:
                    stg = stg_bf[:, (ic % 2) * 128:(ic % 2 + 1) * 128]
                    V((lambda stg_, bank_: lambda v: v.tensor_copy(
                        stg_, bank_[:, 0:128]))(stg, bank),
                      wait=[("mm", C["mm"])] +
                           ([("st", C["st"] - 16)] if ic >= 2 else []),
                      inc=[("ve", 1)])
                    SC((lambda ic_, oc_, stg_: lambda s: s.dma_start(
                        out=wdst[ic_, :, oc_ * 128:(oc_ + 1) * 128], in_=stg_))(ic, oc, stg),
                       wait=[("ve", C["ve"])], inc=[("st", 16)])
                else:
                    V((lambda ic_, oc_, bank_, noc_: lambda v: v.tensor_copy(
                        wdst[:, ic_ * (noc_ * 128) + oc_ * 128: ic_ * (noc_ * 128) + (oc_ + 1) * 128],
                        bank_[:, 0:128]))(ic, oc, bank, noc),
                      wait=[("mm", C["mm"])], inc=[("ve", 1)])
                ve_tr[ic % 2] = C["ve"]
            G(lambda g: g.engine_nop(), wait=[("ve", C["ve"])], inc=[("gc", 1)])
            gc_chain[oc % 2] = C["gc"]

    Q.marks["M1"] = Q.seq
    dequant(Wup, wup_dq, 24, 8)
    if dev:
        dbg_wupA = nc.dram_tensor("dbg_wupA", [128, 24576], BF16, kind="ExternalOutput").ap()
        SC(lambda s: s.dma_start(out=dbg_wupA, in_=wup_dq),
           wait=[("ve", C["ve"])], inc=[("st", 16)])
    dequant(Wgate, wgate_dq, 24, 8)
    dequant(Wdown, wdown_hbm, 8, 24, to_hbm=True)
    deq_scope.close()
    Q.marks["B"] = Q.seq
    Q.marks["B"] = Q.seq
    barrier()

    if dev:
        SC(lambda s: s.dma_start(out=scal_dbg, in_=scal), inc=[("st", 16)])

    mm_xq = [0, 0]
    ve_xqevac = [C["ve"], C["ve"]]
    se_upg = [0, 0]
    mm_upg_last = 0
    for tt in range(NTT):
        for j in range(4):
            jj = (tt * 4 + j) % 2
            xq = x1q_tm[:, jj * D:(jj + 1) * D]
            SY((lambda tt_, j_: lambda s: s.dma_start(
                out=x1rd, in_=x1_hbm[tt_ * 512 + j_ * 128: tt_ * 512 + (j_ + 1) * 128, :]))(tt, j),
               wait=[("se", C["se"])] if (tt + j) > 0 else (),
               inc=[("ld", 16)])
            SC(lambda s: s.activation(y2t, x1rd, AF.Identity,
                                      bias=consts[:, C_U16B:C_U16B + 1],
                                      scale=scal[:, R0:R0 + 1]),
               wait=[("ld", C["ld"]), ("ve", C["ve"])], inc=[("se", 1)])
            V(lambda v: v.tensor_copy(u16a, y2t[:, 0:512]),
              wait=[("se", C["se"])], inc=())
            V(lambda v: v.tensor_copy(u16b, y2t[:, 512:1024]), inc=[("ve", 1)])
            SC((lambda xq_: lambda s: s.activation(
                xq_[:, 0:512], u16a, AF.Identity,
                bias=scal[:, NB0:NB0 + 1], scale=scal[:, S0:S0 + 1]))(xq),
               wait=[("ve", C["ve"])] +
                    ([("mm", mm_xq[jj])] if (tt * 4 + j) >= 2 else []),
               inc=())
            SC((lambda xq_: lambda s: s.activation(
                xq_[:, 512:1024], u16b, AF.Identity,
                bias=scal[:, NB0:NB0 + 1], scale=scal[:, S0:S0 + 1]))(xq),
               inc=[("se", 1)])
            se_xq = C["se"]
            bank = bankT if j % 2 == 0 else bankU
            for dc in range(8):
                T((lambda xq_, dc_, bank_: lambda t: t.transpose(
                    bank_[:, dc_ * 128:(dc_ + 1) * 128],
                    xq_[:, dc_ * 128:(dc_ + 1) * 128], ident_bf))(xq, dc, bank),
                  wait=([("se", se_xq)] if dc == 0 else []) +
                       ([("ve", ve_xqevac[j % 2])] if (dc == 0 and (tt * 4 + j) >= 2) else []),
                  inc=[("mm", 1)])
            mm_xq[jj] = C["mm"]
            V((lambda j_, bank_: lambda v: v.tensor_copy(
                x1q_fm.rearrange("p (c s) -> p c s", c=8)[:, :, j_ * 128:(j_ + 1) * 128],
                bank_.rearrange("p (c s) -> p c s", c=8)))(j, bank),
              wait=[("mm", mm_xq[jj])], inc=[("ve", 1)])
            ve_xqevac[j % 2] = C["ve"]
        ve_xq_fm = C["ve"]

        for wi, (wdq, hbm, mloc) in enumerate(((wup_dq, up_hbm, mupl),
                                               (wgate_dq, gate_hbm, mgatel))):
            for oc in range(24):
                i = wi * 24 + oc
                bank = bankA if i % 2 == 0 else bankB
                for ic in range(8):
                    w = []
                    if i == 0 and ic == 0:
                        w = [("ve", ve_xq_fm)]
                    if ic == 0 and se_upg[i % 2]:
                        w = w + [("se", se_upg[i % 2])]
                    T((lambda wdq_, oc_, ic_, bank_: lambda t: t.matmul(
                        bank_,
                        wdq_[:, ic_ * I3 + oc_ * 128: ic_ * I3 + (oc_ + 1) * 128],
                        x1q_fm[:, ic_ * 512:(ic_ + 1) * 512],
                        start=(ic_ == 0), stop=(ic_ == 7)))(wdq, oc, ic, bank),
                      wait=w, inc=[("mm", 1)] if ic == 7 else ())
                fr = f32a if i % 2 == 0 else f32b
                SC((lambda bank_, f_: lambda s: s.activation(
                    f_, bank_, AF.Identity, bias=0.0, scale=1.0))(bank, fr),
                   wait=[("mm", C["mm"]), ("ve", C["ve"]), ("st", C["st"] - 32)],
                   inc=[("se", 1)])
                se_upg[i % 2] = C["se"]
                SC((lambda tt_, oc_, hbm_, f_: lambda s: s.dma_start(
                    out=hbm_[oc_ * 128:(oc_ + 1) * 128, tt_ * 512:(tt_ + 1) * 512],
                    in_=f_))(tt, oc, hbm, fr),
                   inc=[("st", 16)])
                V((lambda f_: lambda v: v.tensor_reduce(
                    colA, f_, AX.X, ALU.max, apply_absolute_value=True))(fr),
                  wait=[("se", se_upg[i % 2]), ("st", C["st"])], inc=())
                V(lambda v: v.drain(), inc=())
                V((lambda m_: lambda v: v.tensor_tensor(m_, m_, colA, ALU.max))(mloc),
                  inc=())
                V(lambda v: v.drain(), inc=[("ve", 1)])
        mm_upg_last = C["mm"]

    col_to_srow(mupl, 0, first_wait=[("ve", C["ve"])])
    col_to_srow(mgatel, 1)
    allreduce(1, 2)
    bcast_scale(0, SU, RU)
    bcast_scale(1, SG, RG, NBG)
    Q.marks["C"] = Q.seq
    if dev:
        for nm, ap_, shp, dt_ in [("dbg_wup", wup_dq, [128, 24576], BF16),
                                  ("dbg_x1q", x1q_fm, [128, 4096], BF16),
                                  ("dbg_mup", mupl, [128, 1], F32),
                                  ("dbg_mgate", mgatel, [128, 1], F32),
                                  ("dbg_srow", srow, [1, 16], F32)]:
            dbg_t = nc.dram_tensor(nm, shp, dt_, kind="ExternalOutput").ap()
            SC((lambda d_, a_: lambda s: s.dma_start(out=d_, in_=a_))(dbg_t, ap_),
               wait=[("ve", C["ve"]), ("se", C["se"])], inc=[("st", 16)])
    Q.marks["C"] = Q.seq
    barrier()
    ug_scope.close()
    f_scope = ExitStack()
    sbFF = lambda n, s, d=F32: f_scope.enter_context(nc.sbuf_tensor(n, s, d)).ap()
    wdown_dq = sbFF("wdown_dq", [128, 24 * D], BF16)
    o_fm = sbFF("o_fm", [128, 24 * 512], BF16)
    for ic in range(24):
        SY((lambda ic_: lambda s: s.dma_start(
            out=wdown_dq[:, ic_ * D:(ic_ + 1) * D], in_=wdown_hbm[ic_]))(ic),
           inc=[("ld", 16)])

    NOC = I3 // 128
    for tt in range(NTT):
        for oc in range(NOC):
            fr = f32a if oc % 2 == 0 else f32b
            SY((lambda tt_, oc_, fr_: lambda s: s.dma_start(
                out=fr_, in_=gate_hbm[oc_ * 128:(oc_ + 1) * 128,
                                      tt_ * 512:(tt_ + 1) * 512]))(tt, oc, fr),
               wait=[("se", C["se"]), ("ve", C["ve"]), ("st", C["st"])] if (tt + oc) > 0 else (),
               inc=[("ld", 16)])
            SC((lambda fr_: lambda s: s.activation(
                f32c, fr_, AF.Identity, bias=consts[:, C_U16B:C_U16B + 1],
                scale=scal[:, RG:RG + 1]))(fr),
               wait=[("ld", C["ld"]), ("ve", C["ve"])], inc=[("se", 1)])
            V(lambda v: v.tensor_copy(u16a, f32c), wait=[("se", C["se"])], inc=[("ve", 1)])
            SC(lambda s: s.activation(f32c, u16a, AF.Identity,
                                      bias=scal[:, NBG:NBG + 1], scale=scal[:, SG:SG + 1]),
               wait=[("ve", C["ve"])], inc=())
            SC(lambda s: s.activation(f32d, f32c, AF.Sigmoid, bias=0.0, scale=1.0),
               inc=[("se", 1)])
            V(lambda v: v.tensor_scalar(f32d, f32d, 65536.0, 0.5, ALU.mult, ALU.add),
              wait=[("se", C["se"])], inc=())
            V(lambda v: v.tensor_copy(u16b, f32d), inc=())
            V(lambda v: v.scalar_tensor_tensor(f32d, f32c, 1.0 / 65536.0, u16b,
                                               ALU.mult, ALU.mult), inc=())
            V(lambda v: v.tensor_reduce(colA, f32d, AX.X, ALU.max,
                                        apply_absolute_value=True), inc=())
            V(lambda v: v.drain(), inc=())
            V(lambda v: v.tensor_tensor(mtl, mtl, colA, ALU.max), inc=())
            V(lambda v: v.drain(), inc=[("ve", 1)])
            SC((lambda tt_, oc_: lambda s: s.dma_start(
                out=t_hbm[oc_ * 128:(oc_ + 1) * 128, tt_ * 512:(tt_ + 1) * 512],
                in_=f32d))(tt, oc),
               wait=[("ve", C["ve"])], inc=[("st", 16)])

    col_to_srow(mtl, 0, first_wait=[("st", C["st"]), ("ve", C["ve"])])
    allreduce(2, 1)
    bcast_scale(0, S4, R4)
    V(lambda v: v.tensor_tensor(scal[:, S4U:S4U + 1], scal[:, S4:S4 + 1],
                                scal[:, SU:SU + 1], ALU.mult), inc=())
    V(lambda v: v.drain(), inc=())
    V(lambda v: v.tensor_scalar_mul(scal[:, NB4U:NB4U + 1], scal[:, S4U:S4U + 1],
                                    -32768.0), inc=())
    V(lambda v: v.drain(), inc=[("ve", 1)])
    Q.marks["D"] = Q.seq
    Q.marks["D"] = Q.seq
    barrier()

    for tt in range(NTT):
        for oc in range(NOC):
            fr = f32a if oc % 2 == 0 else f32b
            SY((lambda tt_, oc_, fr_: lambda s: s.dma_start(
                out=fr_, in_=t_hbm[oc_ * 128:(oc_ + 1) * 128,
                                   tt_ * 512:(tt_ + 1) * 512]))(tt, oc, fr),
               wait=[("se", C["se"]), ("st", C["st"])] if (tt + oc) > 0 else (),
               inc=[("ld", 16)])
            SY((lambda tt_, oc_: lambda s: s.dma_start(
                out=f32c, in_=up_hbm[oc_ * 128:(oc_ + 1) * 128,
                                     tt_ * 512:(tt_ + 1) * 512]))(tt, oc),
               wait=[("se", C["se"]), ("ve", C["ve"]), ("st", C["st"])] if (tt + oc) > 0 else (),
               inc=[("ld", 16)])
            ld_e = C["ld"]
            SC((lambda fr_: lambda s: s.activation(
                f32d, fr_, AF.Identity, bias=consts[:, C_U16B:C_U16B + 1],
                scale=scal[:, R4:R4 + 1]))(fr),
               wait=[("ld", ld_e), ("ve", C["ve"])], inc=[("se", 1)])
            V(lambda v: v.tensor_copy(u16a, f32d), wait=[("se", C["se"])], inc=[("ve", 1)])
            SC(lambda s: s.activation(f32d, u16a, AF.Identity,
                                      bias=scal[:, NB4U:NB4U + 1],
                                      scale=scal[:, S4U:S4U + 1]),
               wait=[("ve", C["ve"])], inc=())
            SC(lambda s: s.activation(f32c, f32c, AF.Identity,
                                      bias=consts[:, C_U16B:C_U16B + 1],
                                      scale=scal[:, RU:RU + 1]), inc=[("se", 1)])
            V(lambda v: v.tensor_copy(u16b, f32c), wait=[("se", C["se"])], inc=())
            V(lambda v: v.scalar_tensor_tensor(f32c, u16b, -32768.0, f32d,
                                               ALU.add, ALU.mult), inc=())
            V(lambda v: v.tensor_reduce(colA, f32c, AX.X, ALU.max,
                                        apply_absolute_value=True), inc=())
            V(lambda v: v.drain(), inc=())
            V(lambda v: v.tensor_tensor(mol, mol, colA, ALU.max), inc=())
            V(lambda v: v.drain(), inc=[("ve", 1)])
            SC((lambda tt_, oc_: lambda s: s.dma_start(
                out=op_hbm[oc_ * 128:(oc_ + 1) * 128, tt_ * 512:(tt_ + 1) * 512],
                in_=f32c))(tt, oc),
               wait=[("ve", C["ve"])], inc=[("st", 16)])

    col_to_srow(mol, 0, first_wait=[("st", C["st"]), ("ve", C["ve"])])
    allreduce(3, 1)
    bcast_scale(0, S5, R5, NB5)
    Q.marks["E"] = Q.seq
    Q.marks["E"] = Q.seq
    barrier()

    mm_down_prev = 0
    for tt in range(NTT):
        for oc in range(NOC):
            fr = f32a if oc % 2 == 0 else f32b
            SY((lambda tt_, oc_, fr_: lambda s: s.dma_start(
                out=fr_, in_=op_hbm[oc_ * 128:(oc_ + 1) * 128,
                                    tt_ * 512:(tt_ + 1) * 512]))(tt, oc, fr),
               wait=[("se", C["se"])] if (tt + oc) > 0 else (),
               inc=[("ld", 16)])
            SC((lambda fr_: lambda s: s.activation(
                f32c, fr_, AF.Identity, bias=consts[:, C_U16B:C_U16B + 1],
                scale=scal[:, R5:R5 + 1]))(fr),
               wait=[("ld", C["ld"]), ("ve", C["ve"])], inc=[("se", 1)])
            V(lambda v: v.tensor_copy(u16a, f32c),
              wait=[("se", C["se"])], inc=[("ve", 1)])
            SC((lambda oc_: lambda s: s.activation(
                o_fm[:, oc_ * 512:(oc_ + 1) * 512], u16a, AF.Identity,
                bias=scal[:, NB5:NB5 + 1], scale=scal[:, S5:S5 + 1]))(oc),
               wait=[("ve", C["ve"])] +
                    ([("mm", mm_down_prev)] if (tt > 0 and oc == 0) else []),
               inc=[("se", 1)])
        se_o = C["se"]

        for j in range(4):
            tok0 = tt * 512 + j * 128
            b_idx, row0 = tok0 // S, tok0 % S
            SY((lambda b_, r_: lambda s: s.dma_start(
                out=x1rd, in_=x1_hbm[b_ * S + r_: b_ * S + r_ + 128, :]))(b_idx, row0),
               wait=[("ve", C["ve"]), ("st", C["st"])], inc=[("ld", 16)])
            ld_x1 = C["ld"]
            for dt in range(2):
                for oc in range(NOC):
                    w = []
                    if dt == 0 and oc == 0:
                        w = [("se", se_o), ("ve", C["ve"])]
                    T((lambda j_, dt_, oc_: lambda t: t.matmul(
                        bankC[:, dt_ * 512:(dt_ + 1) * 512],
                        o_fm[:, oc_ * 512 + j_ * 128: oc_ * 512 + (j_ + 1) * 128],
                        wdown_dq[:, oc_ * D + dt_ * 512: oc_ * D + dt_ * 512 + 512],
                        start=(oc_ == 0), stop=(oc_ == 23)))(j, dt, oc),
                      wait=w, inc=[("mm", 1)] if oc == 23 else ())
            mm_down_prev = C["mm"]
            V(lambda v: v.scalar_tensor_tensor(
                r1f, bankC, 1.0, x1rd, ALU.mult, ALU.add),
              wait=[("mm", mm_down_prev), ("ld", ld_x1)], inc=[("ve", 1)])
            V(lambda v: v.reduce_sum(lncol[:, 0:1], r1f, AX.X), inc=())
            V(lambda v: v.drain(), inc=[("ve", 1)])
            SC(lambda s: s.activation(y2t, r1f, AF.Square, bias=0.0,
                                      scale=1.0),
               wait=[("ve", C["ve"])], inc=[("se", 1)])
            V(lambda v: v.reduce_sum(lncol[:, 2:3], y2t, AX.X),
              wait=[("se", C["se"])], inc=())
            V(lambda v: v.drain(), inc=())
            V(lambda v: v.tensor_scalar_mul(lncol[:, 1:2], lncol[:, 0:1], 1.0 / D), inc=())
            V(lambda v: v.drain(), inc=())
            V(lambda v: v.tensor_scalar(lncol[:, 3:4], lncol[:, 2:3], 1.0 / D,
                                        consts[:, C_EPS:C_EPS + 1], ALU.mult, ALU.add),
              inc=())
            V(lambda v: v.drain(), inc=())
            V(lambda v: v.tensor_mul(colA, lncol[:, 1:2], lncol[:, 1:2]), inc=())
            V(lambda v: v.drain(), inc=())
            V(lambda v: v.tensor_tensor(lncol[:, 3:4], lncol[:, 3:4], colA,
                                        ALU.subtract), inc=())
            V(lambda v: v.drain(), inc=[("ve", 1)])
            SC(lambda s: s.activation(lncol[:, 4:5], lncol[:, 3:4], AF.Ln,
                                      bias=0.0, scale=1.0),
               wait=[("ve", C["ve"])], inc=())
            SC(lambda s: s.activation(lncol[:, 4:5], lncol[:, 4:5], AF.Exp,
                                      bias=0.0, scale=-0.5), inc=[("se", 1)])
            V(lambda v: v.engine_nop(), wait=[("se", C["se"])], inc=())
            for _ in range(2):
                V(lambda v: v.tensor_mul(colA, lncol[:, 4:5], lncol[:, 4:5]), inc=())
                V(lambda v: v.drain(), inc=())
                V(lambda v: v.tensor_tensor(colA, colA, lncol[:, 3:4], ALU.mult), inc=())
                V(lambda v: v.drain(), inc=())
                V(lambda v: v.tensor_scalar(colA, colA, -0.5, 1.5, ALU.mult, ALU.add), inc=())
                V(lambda v: v.drain(), inc=())
                V(lambda v: v.tensor_tensor(lncol[:, 4:5], lncol[:, 4:5], colA, ALU.mult), inc=())
                V(lambda v: v.drain(), inc=())
            V(lambda v: v.tensor_mul(colB, lncol[:, 1:2], lncol[:, 4:5]), inc=())
            V(lambda v: v.drain(), inc=())
            V(lambda v: v.tensor_scalar_mul(lncol[:, 5:6], colB, -1.0),
              inc=())
            V(lambda v: v.drain(), inc=[("ve", 1)])
            SC(lambda s: s.activation(x1rd, r1f, AF.Identity,
                                      bias=lncol[:, 5:6], scale=lncol[:, 4:5]),
               wait=[("ve", C["ve"])], inc=[("se", 1)])
            SC((lambda b_, r_: lambda s: s.dma_start(
                out=out[b_, r_:r_ + 128, :], in_=x1rd))(b_idx, row0),
               inc=[("st", 16)])
            SC(lambda s: None, wait=[("st", C["st"])])

    st_final = C["st"]
    for e in ("tensor", "vector", "scalar", "gpsimd", "sync"):
        Q.op(e, lambda eng: None, wait=[("st", st_final)])

    cut = None if cut_phase is None else Q.marks[cut_phase]
    with nc.Block() as block:
        Q.emit(block, cut=cut)
    return nc


_NC_CACHE = {}


def _get_nc(dev=False):
    if dev not in _NC_CACHE:
        _NC_CACHE[dev] = build(dev)
    return _NC_CACHE[dev]


def make_inputs(inputs, core):
    d = {
        "x": inputs["L0_attn_x"][core * NB:(core + 1) * NB],
        "Wq": inputs["L0_attn_Wq"], "bq": inputs["L0_attn_bq"],
        "Wk": inputs["L0_attn_Wk"], "bk": inputs["L0_attn_bk"],
        "Wv": inputs["L0_attn_Wv"], "bv": inputs["L0_attn_bv"],
        "Wo": inputs["L0_attn_Wo"], "bo": inputs["L0_attn_bo"],
        "Wup": inputs["L0_ffn_w_up"], "Wgate": inputs["L0_ffn_w_gate"],
        "Wdown": inputs["L0_ffn_w_down"],
        "ident": np.eye(128, dtype=np.float32),
        "ones128": np.ones((128, 128), np.float32),
        "onesrow": np.ones((1, 128), np.float32),
        "consts": np.tile(np.array([[EPS, 32767.0, 7.0, U16B, 0, 0, 0, 0]],
                                   np.float32), (128, 1)),
    }
    return {k: np.ascontiguousarray(np.asarray(v, dtype=np.float32)) for k, v in d.items()}


def kernel(L0_attn_x, L0_attn_Wq, L0_attn_bq, L0_attn_Wk, L0_attn_bk,
           L0_attn_Wv, L0_attn_bv, L0_attn_Wo, L0_attn_bo,
           L0_ffn_w_gate, L0_ffn_w_up, L0_ffn_w_down):
    inputs = dict(L0_attn_x=L0_attn_x, L0_attn_Wq=L0_attn_Wq, L0_attn_bq=L0_attn_bq,
                  L0_attn_Wk=L0_attn_Wk, L0_attn_bk=L0_attn_bk, L0_attn_Wv=L0_attn_Wv,
                  L0_attn_bv=L0_attn_bv, L0_attn_Wo=L0_attn_Wo, L0_attn_bo=L0_attn_bo,
                  L0_ffn_w_gate=L0_ffn_w_gate, L0_ffn_w_up=L0_ffn_w_up,
                  L0_ffn_w_down=L0_ffn_w_down)
    nc = _get_nc(False)
    in_maps = [make_inputs(inputs, c) for c in range(NCORES)]
    res = run_bass_kernel_spmd(nc, in_maps, core_ids=list(range(NCORES)))
    return np.concatenate([res.results[c]["out"] for c in range(NCORES)], axis=0)
